# version 10
# speedup vs baseline: 1.1460x; 1.0010x over previous
import contextlib
import numpy as np
import ml_dtypes
import concourse.bass as bass
import concourse.mybir as mybir
from concourse.bass_utils import run_bass_kernel_spmd

F32 = mybir.dt.float32
BF16 = mybir.dt.bfloat16
AX = mybir.AxisListType.X
ALU = mybir.AluOpType
ACT = mybir.ActivationFunctionType
BF = ml_dtypes.bfloat16

B, T, D = 2, 8192, 2048
H, HKV = 16, 4
DG = 32
LW = 1024
NEG = -1e9
BIG = 1e30
I_FF = 5632
O1 = 2 * I_FF
NSLOT = 16
NCH = D // 128
NPOOL = 112
KEYPAD = 128
MAGIC = float(3 * 2 ** 22)
ROPE_BASE = 10000.0
LN_EPS = 1e-5
Q_EPS = 1e-8
QMAX = 127.0
ISQ32 = float(1.0 / np.sqrt(32.0))
NPAIR = 11
NW2 = I_FF // 128
HEADS_A = [0, 1, 2, 3, 8, 9, 10, 11]
HEADS_B = [4, 5, 6, 7, 12, 13, 14, 15]
HEADS = [h for c in range(8) for h in (HEADS_A[c], HEADS_B[c])]
NW1 = O1 // 128


def _pm(m):
    return NPOOL if m == 0 else min(8 * m + 8, NPOOL)


def _wm(m):
    return {0: LW, 14: 512, 15: LW}.get(m, 0)


def _ks(m):
    P, W = _pm(m), _wm(m)
    return P if W == 0 else 512 + W


KSTORE = [_ks(m) for m in range(NSLOT)]
MOFF = np.concatenate([[0], np.cumsum(KSTORE)]).astype(int)
MASKW = int(MOFF[-1])



def _perm_tiles(j):
    return [4 * g + ((j + k) % 4) for g in range(16) for k in range(4)]


def _perm_tokens(j):
    return np.concatenate([np.arange(t * 128, t * 128 + 128) for t in _perm_tiles(j)])


def _rope_tables(pos):
    n = len(pos)
    cos = np.ones((128, n), np.float32)
    sin = np.zeros((128, n), np.float32)
    inv_freq = ROPE_BASE ** (-(np.arange(0, DG, 2, dtype=np.float64) / DG))
    ang = pos.astype(np.float64)[None, :] * inv_freq[:, None]
    c = np.cos(ang).astype(np.float32)
    s = np.sin(ang).astype(np.float32)
    for grp in range(2):
        base = 64 * grp + 32
        cos[base:base + 16] = c
        cos[base + 16:base + 32] = c
        sin[base:base + 16] = -s
        sin[base + 16:base + 32] = s
    return cos, sin


def _p2swap():
    m = np.zeros((128, 128), np.float32)
    for p in range(128):
        l = p % 64
        src = p if l < 32 else (p + 16 if l < 48 else p - 16)
        m[src, p] = 1.0
    return m.astype(BF)


def _host_prepare(inputs):
    x = np.asarray(inputs["L0_attn_x"], np.float32)
    wq_sem = np.asarray(inputs["L0_attn_Wq_sem"], np.float32)
    wk_sem = np.asarray(inputs["L0_attn_Wk_sem"], np.float32)
    wq_geo = np.asarray(inputs["L0_attn_Wq_geo"], np.float32)
    wk_geo = np.asarray(inputs["L0_attn_Wk_geo"], np.float32)
    wv = np.asarray(inputs["L0_attn_Wv"], np.float32)
    wo = np.asarray(inputs["L0_attn_Wo"], np.float32)
    lsc = np.asarray(inputs["L0_attn_logit_scale"], np.float32)
    w1 = np.asarray(inputs["L0_ffn_w_gate_up"], np.float32)
    w2 = np.asarray(inputs["L0_ffn_w_down"], np.float32)

    wq_comb = np.empty((D, 1024), np.float32)
    wo_perm = np.empty_like(wo)
    for pos, h in enumerate(HEADS):
        wq_comb[:, 64 * pos:64 * pos + 32] = wq_sem[:, 32 * h:32 * h + 32]
        wq_comb[:, 64 * pos + 32:64 * pos + 64] = wq_geo[:, 32 * h:32 * h + 32]
        wo_perm[64 * pos:64 * pos + 64] = wo[64 * h:64 * h + 64]
    wo = wo_perm
    wkv_comb = np.empty((D, 512), np.float32)
    for kv in range(HKV):
        wkv_comb[:, 64 * kv:64 * kv + 32] = wk_sem[:, 32 * kv:32 * kv + 32]
        wkv_comb[:, 64 * kv + 32:64 * kv + 64] = wk_geo[:, 32 * kv:32 * kv + 32]
    wkv_comb[:, 256:512] = wv

    consts = np.zeros((128, 8), np.float32)
    consts[:, 1] = 1.0
    consts[:, 2] = -1.0
    consts[:, 3] = MAGIC
    consts[:, 4] = -MAGIC
    shared = dict(
        wq_comb=wq_comb, wkv_comb=wkv_comb, wo=wo, w1nat=w1,
        w1T=np.ascontiguousarray(w1.T), w2nat=w2,
        w2T=np.ascontiguousarray(w2.T), consts=consts,
        ident=np.eye(128, dtype=BF), identf=np.eye(128, dtype=np.float32),
        ones_col=np.ones((128, 1), BF),
        ones_row=np.ones((1, 128), np.float32), p2=_p2swap(),
        lsc=lsc.reshape(1, H).astype(np.float32))

    in_maps = []
    for core in range(8):
        b, j = divmod(core, 4)
        ptok = _perm_tokens(j)
        xp = x[b][ptok]
        xT = np.ascontiguousarray(xp.T)
        own = np.concatenate([np.arange(512 * g, 512 * g + 128) for g in range(16)])
        x_own = np.ascontiguousarray(xp[own])
        cq, sq = _rope_tables(ptok[own])
        ck, sk = _rope_tables(ptok)

        pt = _perm_tiles(j)
        kpos_pool = np.empty(NPOOL, np.int64)
        for lb in range(NPOOL):
            tt = pt[lb // 2]
            kpos_pool[lb] = 64 * (2 * tt + (lb % 2)) + 63
        kpos_win = ptok[7168:8192]
        masks = np.empty((128, MASKW), np.float32)
        for m in range(NSLOT):
            p = 128 * (4 * m + j) + np.arange(128)
            P, W = _pm(m), _wm(m)
            tile = np.full((128, KSTORE[m]), NEG, np.float32)
            tile[:, 0:P] = np.where(kpos_pool[None, :P] <= p[:, None], BIG, NEG)
            if W > 0:
                tile[:, 512:512 + W] = np.where(
                    kpos_win[None, :W] <= p[:, None], BIG, NEG)
            masks[:, MOFF[m]:MOFF[m] + KSTORE[m]] = tile

        im = dict(shared)
        im.update(xT=xT, x_own=x_own, cosq=cq, sinq=sq, cosk=ck, sink=sk,
                  masks=masks)
        in_maps.append(im)
    return in_maps



ENGINES = ("sync", "gpsimd", "tensor", "vector", "scalar")


class Sem:
    __slots__ = ("h", "n")

    def __init__(self, h):
        self.h = h
        self.n = 0


class Sched:
    def __init__(self, nc, stack):
        self.nc = nc
        self.stack = stack
        self.q = {e: [] for e in ENGINES}

    def sem(self, name):
        return Sem(self.stack.enter_context(self.nc.semaphore(name)))

    def op(self, eng, build, waits=(), inc=None, amt=1):
        wl = []
        for w in waits:
            if isinstance(w, tuple):
                s, v = w
                wl.append((s.h, int(v)))
            else:
                wl.append((w.h, int(w.n)))
        if inc is not None:
            inc.n += amt
        ih = inc.h if inc is not None else None

        def f(engine, build=build, wl=wl, ih=ih, amt=amt):
            for h, v in wl:
                if v > 0:
                    engine.wait_ge(h, v)
            ins = build(engine)
            if ih is not None:
                ins.then_inc(ih, amt)

        self.q[eng].append(f)

    def raw_wait(self, eng, sem, val=None):
        v = int(sem.n if val is None else val)
        if v > 0:
            self.q[eng].append(lambda engine, h=sem.h, v=v: engine.wait_ge(h, v))

    def barrier(self, sems, engines=ENGINES):
        snap = [(s, s.n) for s in sems if s.n > 0]
        for e in engines:
            for s, v in snap:
                self.raw_wait(e, s, v)

    def finalize(self, block):
        def run(engine, fns):
            for f in fns:
                f(engine)
        block.sync(lambda e: run(e, self.q["sync"]))
        block.gpsimd(lambda e: run(e, self.q["gpsimd"]))
        block.tensor(lambda e: run(e, self.q["tensor"]))
        block.vector(lambda e: run(e, self.q["vector"]))
        block.scalar(lambda e: run(e, self.q["scalar"]))



def build_program(stage="full"):
    nc = bass.Bass("TRN2", target_bir_lowering=False)

    d_xT = nc.dram_tensor("xT", [D, T], F32, kind="ExternalInput").ap()
    d_xown = nc.dram_tensor("x_own", [2048, D], F32, kind="ExternalInput").ap()
    d_wq = nc.dram_tensor("wq_comb", [D, 1024], F32, kind="ExternalInput").ap()
    d_wkv = nc.dram_tensor("wkv_comb", [D, 512], F32, kind="ExternalInput").ap()
    d_wo = nc.dram_tensor("wo", [1024, D], F32, kind="ExternalInput").ap()
    d_w1nat = nc.dram_tensor("w1nat", [O1, D], F32, kind="ExternalInput").ap()
    d_w1T = nc.dram_tensor("w1T", [D, O1], F32, kind="ExternalInput").ap()
    d_w2nat = nc.dram_tensor("w2nat", [D, I_FF], F32, kind="ExternalInput").ap()
    d_w2T = nc.dram_tensor("w2T", [I_FF, D], F32, kind="ExternalInput").ap()
    d_consts = nc.dram_tensor("consts", [128, 8], F32, kind="ExternalInput").ap()
    d_ident = nc.dram_tensor("ident", [128, 128], BF16, kind="ExternalInput").ap()
    d_identf = nc.dram_tensor("identf", [128, 128], F32, kind="ExternalInput").ap()
    d_onesc = nc.dram_tensor("ones_col", [128, 1], BF16, kind="ExternalInput").ap()
    d_onesr = nc.dram_tensor("ones_row", [1, 128], F32, kind="ExternalInput").ap()
    d_p2 = nc.dram_tensor("p2", [128, 128], BF16, kind="ExternalInput").ap()
    d_lsc = nc.dram_tensor("lsc", [1, H], F32, kind="ExternalInput").ap()
    d_cosq = nc.dram_tensor("cosq", [128, 2048], F32, kind="ExternalInput").ap()
    d_sinq = nc.dram_tensor("sinq", [128, 2048], F32, kind="ExternalInput").ap()
    d_cosk = nc.dram_tensor("cosk", [128, T], F32, kind="ExternalInput").ap()
    d_sink = nc.dram_tensor("sink", [128, T], F32, kind="ExternalInput").ap()
    d_masks = nc.dram_tensor("masks", [128, MASKW], F32, kind="ExternalInput").ap()

    scratch_kind = "ExternalOutput" if stage == "debug" else "Internal"
    d_y = nc.dram_tensor("y_scr", [2048, D], F32, kind=scratch_kind).ap()
    d_qxt = nc.dram_tensor("qxt_scr", [2048, D], BF16, kind=scratch_kind).ap()
    d_ht = nc.dram_tensor("ht_scr", [2048, I_FF], F32, kind=scratch_kind).ap()
    d_qw2 = nc.dram_tensor("qw2_scr", [I_FF, D], BF16, kind=scratch_kind).ap()
    d_ws1 = nc.dram_tensor("ws1_scr", [1, O1], F32, kind=scratch_kind).ap()
    d_ws2 = nc.dram_tensor("ws2_scr", [1, D], F32, kind=scratch_kind).ap()
    d_out = nc.dram_tensor("out", [2048, D], F32, kind="ExternalOutput").ap()
    d_dbg = None
    if stage == "debug":
        d_dbg = nc.dram_tensor("dbg", [128, 16 * 13], F32, kind="ExternalOutput").ap()
    d_dbg4 = None
    if stage == "debug":
        d_dbg4 = nc.dram_tensor("dbg4", [128, 512 * 22], F32,
                                kind="ExternalOutput").ap()
    d_dbg3 = None
    if stage in ("y", "debug"):
        d_dbg3 = nc.dram_tensor("dbg3", [1, 16 * 512 * 3], F32,
                                kind="ExternalOutput").ap()
    d_dbg2 = None
    if stage in ("y", "debug"):
        d_dbg2 = nc.dram_tensor("dbg2", [128, 2 * 1152 + 9 * 256 + 8 * 2048],
                                BF16, kind="ExternalOutput").ap()

    stack = contextlib.ExitStack()
    with stack:
        S = Sched(nc, stack)
        sball = lambda name, shape, dt=F32: stack.enter_context(
            nc.sbuf_tensor("sb_" + name, shape, dt))

        t_consts = sball("consts", [128, 8])
        t_ident = sball("ident", [128, 128], BF16)
        t_identf = sball("identf", [128, 128])
        t_onesc = sball("onesc", [128, 1], BF16)
        t_onesr = sball("onesr", [1, 128])
        t_p2 = sball("p2", [128, 128], BF16)
        t_cbc = sball("cbc", [128, H])
        t_k2 = [sball(f"k2_{i}", [128, KEYPAD + LW], BF16) for i in range(2)]
        t_v2km = [sball(f"v2km_{i}", [128, 256], BF16) for i in range(9)]
        t_xs = sball("xs_all", [128, NSLOT])
        t_habs = sball("habs", [128, NSLOT * NPAIR])
        t_hsc = sball("hsc_all", [128, NSLOT])
        t_call = sball("c_all", [128, NSLOT])

        nc.const_aps.register(F32, 0.0, t_consts[:, 0:1])
        nc.const_aps.register(F32, 1.0, t_consts[:, 1:2])
        nc.const_aps.register(F32, -1.0, t_consts[:, 2:3])
        nc.const_aps.register(F32, MAGIC, t_consts[:, 3:4])
        nc.const_aps.register(F32, -MAGIC, t_consts[:, 4:5])

        s_ld = S.sem("ld")
        s_gld = S.sem("gld")
        s_c0 = S.sem("c0")
        s_wkv = S.sem("wkvl")
        s_xtsS = [S.sem(f"xts{i}") for i in range(2)]
        s_cskS = S.sem("cskl")
        s_wqL = S.sem("wql")
        s_woL = S.sem("wol")
        s_csqS = S.sem("csql")
        s_maskS = [S.sem(f"maskl{i}") for i in range(2)]
        s_xotS = [S.sem(f"xotl{i}") for i in range(2)]
        s_wscL = S.sem("wscl")
        s_ystS = [S.sem(f"yst{i}") for i in range(2)]
        s_qxst = S.sem("qxst")
        s_wrow = S.sem("wrowst")
        s_dbgst = S.sem("dbgst")
        s_qxl = S.sem("qxl")
        s_w1tS = [S.sem(f"w1t{i}") for i in range(4)]
        s_wr1L = S.sem("wr1l")
        s_w2tL = S.sem("w2tl")
        s_qw2w = S.sem("qw2w")
        s_htS = [S.sem(f"htst{i}") for i in range(2)]
        s_htrS = [S.sem(f"htr{i}") for i in range(2)]
        s_qw2L = S.sem("qw2l")
        s_yrdS = [S.sem(f"yrd{i}") for i in range(4)]
        s_outS = [S.sem(f"outst{i}") for i in range(4)]
        s_wr2L = S.sem("wr2l")
        s_pe = S.sem("pe")
        s_ev = S.sem("ev")
        s_evS = S.sem("evS")
        s_sq = S.sem("sq")
        s_sqm = S.sem("sqm")
        s_row = S.sem("row")
        s_rowS = S.sem("rowS")
        s_st = S.sem("st")
        s_out = S.sem("out")
        s_wsc = S.sem("wsc")

        def dma(eng, out_ap, in_ap, sem, waits=()):
            S.op(eng, lambda e, o=out_ap, i=in_ap: e.dma_start(out=o, in_=i),
                 waits=waits, inc=sem, amt=16)

        es_kv = contextlib.ExitStack()
        es_xt = contextlib.ExitStack()
        es_qs = contextlib.ExitStack()

        t_xtown = es_xt.enter_context(
            nc.sbuf_tensor("sb_xtown", [128, NCH * 2048], BF16))
        t_rown = es_xt.enter_context(nc.sbuf_tensor("sb_rown", [1, 2048], F32))
        t_nmown = es_xt.enter_context(nc.sbuf_tensor("sb_nmown", [1, 2048], BF16))
        t_wkv = es_kv.enter_context(nc.sbuf_tensor("sb_wkv", [128, NCH * 512], BF16))
        t_wsumkv = es_kv.enter_context(nc.sbuf_tensor("sb_wsumkv", [1, 512], BF16))
        with contextlib.ExitStack() as ph0:
            t_lsc = ph0.enter_context(nc.sbuf_tensor("sb_lsc", [1, H], F32))
            t_crow = ph0.enter_context(nc.sbuf_tensor("sb_crow", [1, H], F32))
            p_c = ph0.enter_context(nc.psum_tensor("p_c", [128, H], F32))
            p_w = ph0.enter_context(nc.psum_tensor("p_w", [1, 512], F32))

            for ap_, src in [(t_consts[:], d_consts), (t_ident[:], d_ident),
                             (t_identf[:], d_identf), (t_onesc[:], d_onesc),
                             (t_onesr[:], d_onesr), (t_p2[:], d_p2),
                             (t_lsc[:], d_lsc)]:
                dma("sync", ap_, src, s_c0)
            for c in range(NCH):
                dma("gpsimd", t_wkv[:, 512 * c:512 * c + 512],
                    d_wkv[128 * c:128 * c + 128, :], s_wkv)
            ld0 = (s_c0, s_c0.n)
            gld0 = (s_wkv, s_wkv.n)

            for i in range(2):
                S.op("vector", lambda e, t=t_k2[i]: e.memset(t[:, NPOOL:KEYPAD], 0.0),
                     inc=s_ev)
            S.op("vector", lambda e: e.memset(t_v2km[0][96:128, :], 0.0), inc=s_ev)

            S.op("scalar", lambda e: e.activation(t_crow[:], t_lsc[:], ACT.Exp),
                 waits=[ld0], inc=s_evS)
            S.op("vector", lambda e: e.tensor_scalar(
                t_crow[:], t_crow[:], ISQ32, None, ALU.mult),
                waits=[(s_evS, s_evS.n)], inc=s_ev)
            S.op("tensor", lambda e: e.matmul(p_c[:], t_onesr[:], t_crow[:],
                                              start=True, stop=True),
                 waits=[ld0, (s_ev, s_ev.n)], inc=s_pe)
            S.op("vector", lambda e: e.tensor_copy(t_cbc[:], p_c[:]),
                 waits=[(s_pe, s_pe.n)], inc=s_ev)

            for c in range(NCH):
                S.op("tensor", lambda e, c=c: e.matmul(
                    p_w[:], t_onesc[:], t_wkv[:, 512 * c:512 * c + 512],
                    start=(c == 0), stop=(c == NCH - 1)),
                    waits=([ld0, gld0] if c == 0 else []),
                    inc=(s_pe if c == NCH - 1 else None))
            S.op("vector", lambda e: e.tensor_copy(t_wsumkv[:], p_w[:]),
                 waits=[(s_pe, s_pe.n)], inc=s_ev)
            S.barrier([s_pe, s_ev, s_evS])

        with contextlib.ExitStack() as ph1:
            sb1 = lambda name, shape, dt=F32: ph1.enter_context(
                nc.sbuf_tensor("sb_" + name, shape, dt))
            pb1 = lambda name, shape: ph1.enter_context(
                nc.psum_tensor(name, shape, F32))
            t_xts = [sb1(f"xts{i}", [128, NCH * 512], BF16) for i in range(2)]
            t_sqb = [sb1(f"sqb{i}", [128, 512], BF16) for i in range(2)]
            t_kvs = [sb1(f"kvs{i}", [128, 4 * 512], BF16) for i in range(2)]
            t_csk = sb1("cskb", [128, 1024])
            t_rbc = sb1("rbc", [128, 512])
            t_rows = sb1("rows", [1, 2560])
            t_nmrow = sb1("nmrow", [1, 512], BF16)
            t_tmp1 = sb1("tmp1", [128, 512])
            t_tmp2 = sb1("tmp2", [128, 512])
            t_k2f = [sb1(f"k2f{i}", [128, NPOOL]) for i in range(2)]
            t_v2f = [sb1(f"v2f{i}", [128, NPOOL]) for i in range(2)]
            t_v2ps = [sb1(f"v2ps{i}", [128, NPOOL], BF16) for i in range(2)]
            t_dbg3 = True if d_dbg3 is not None else None
            p_sum = pb1("p_sum", [1, 512])
            p_sq = pb1("p_sq", [1, 512])
            p_proj = [pb1(f"p_proj{i}", [128, 512]) for i in range(4)]
            p_rbc = pb1("p_rbc", [128, 512])
            p_sw = pb1("p_sw", [128, 512])

            R_MEAN = t_rows[:, 0:512]
            R_VEPS = t_rows[:, 512:1024]
            R_RCP = t_rows[:, 1024:1536]
            R_RSTD = t_rows[:, 1536:2048]
            R_MM2 = t_rows[:, 2048:2560]

            hist = []
            csk_used = 0
            for g in range(16):
                par = g % 2
                xts, kvs = t_xts[par], t_kvs[par]
                bufw = [] if g < 2 else [(s_pe, hist[g - 2][0]),
                                         (s_ev, hist[g - 2][1]),
                                         (s_sq, hist[g - 2][2])]
                sx = s_xtsS[par]
                for c in range(NCH):
                    S.op("gpsimd", lambda e, c=c, xts=xts, g=g: e.dma_start(
                        out=xts[:, 512 * c:512 * c + 512],
                        in_=d_xT[128 * c:128 * c + 128, 512 * g:512 * g + 512]),
                        waits=(bufw if c == 0 else []), inc=sx, amt=16)
                dma("sync", t_csk[:, 0:512], d_cosk[:, 512 * g:512 * g + 512],
                    s_cskS, waits=[(s_ev, csk_used)])
                dma("sync", t_csk[:, 512:1024], d_sink[:, 512 * g:512 * g + 512],
                    s_cskS)
                csk_rdy = (s_cskS, s_cskS.n)

                for c in range(NCH):
                    S.op("tensor", lambda e, c=c, xts=xts: e.matmul(
                        p_sum[:], t_onesc[:], xts[:, 512 * c:512 * c + 512],
                        start=(c == 0), stop=(c == NCH - 1)),
                        waits=([(sx, sx.n), (s_row, s_row.n)]
                               if c == 0 else [(sx, sx.n)]),
                        inc=(s_pe if c == NCH - 1 else None))
                sum_done = (s_pe, s_pe.n)
                ev_before = s_ev.n
                for mc in range(4):
                    pp = p_proj[mc]
                    for c in range(NCH):
                        S.op("tensor", lambda e, mc=mc, c=c, pp=pp, xts=xts: e.matmul(
                            pp[:], t_wkv[:, 512 * c + 128 * mc:512 * c + 128 * mc + 128],
                            xts[:, 512 * c:512 * c + 512],
                            start=(c == 0), stop=False),
                            waits=([(sx, sx.n), (s_ev, ev_before)]
                                   if c == 0 else []))
                for c in range(NCH):
                    sqb = t_sqb[c % 2]
                    sqw = [(sx, sx.n)]
                    if c >= 2:
                        sqw.append((s_sqm, s_sqm.n - 1))
                    elif g >= 1:
                        sqw.append((s_sqm, hist[g - 1][3]))
                    S.op("scalar", lambda e, c=c, xts=xts, sqb=sqb: e.activation(
                        sqb[:], xts[:, 512 * c:512 * c + 512], ACT.Square),
                        waits=sqw, inc=s_sq)
                    S.op("tensor", lambda e, c=c, sqb=sqb: e.matmul(
                        p_sq[:], t_onesc[:], sqb[:],
                        start=(c == 0), stop=(c == NCH - 1)),
                        waits=[(s_sq, s_sq.n)] +
                              ([(s_row, s_row.n)] if c == 0 else []),
                        inc=s_sqm)
                sq_done = (s_sqm, s_sqm.n)

                S.op("vector", lambda e: e.tensor_scalar(
                    R_MEAN, p_sum[:], 1.0 / D, None, ALU.mult),
                    waits=[sum_done, sq_done], inc=s_ev)
                S.op("vector", lambda e: e.tensor_tensor(
                    R_MM2, R_MEAN, R_MEAN, ALU.mult),
                    waits=[(s_ev, s_ev.n)], inc=s_ev)
                S.op("vector", lambda e: e.scalar_tensor_tensor(
                    R_VEPS, p_sq[:], 1.0 / D, R_MM2, ALU.mult, ALU.subtract),
                    waits=[(s_ev, s_ev.n)], inc=s_ev)
                S.op("vector", lambda e: e.tensor_scalar(
                    R_VEPS, R_VEPS, LN_EPS, None, ALU.add),
                    waits=[(s_ev, s_ev.n)], inc=s_ev)
                S.op("vector", lambda e: e.reciprocal(R_RCP, R_VEPS),
                    waits=[(s_ev, s_ev.n)], inc=s_row)
                S.op("scalar", lambda e: e.activation(R_RSTD, R_RCP, ACT.Sqrt),
                     waits=[(s_row, s_row.n)], inc=s_rowS)
                rstd_rdy = (s_rowS, s_rowS.n)
                if t_dbg3 is not None:
                    for di, src in enumerate((R_MEAN, R_VEPS, R_RSTD)):
                        S.op("sync", lambda e, g=g, di=di, src=src: e.dma_start(
                            out=d_dbg3[0:1, 1536 * g + 512 * di:
                                       1536 * g + 512 * di + 512],
                            in_=src), waits=[rstd_rdy], inc=s_dbgst, amt=16)
                S.op("vector", lambda e: e.tensor_scalar(
                    t_nmrow[:], R_MEAN, -1.0, None, ALU.mult), inc=s_row)
                nm_rdy = (s_row, s_row.n)
                S.op("tensor", lambda e: e.matmul(
                    p_rbc[:], t_onesr[:], R_RSTD, start=True, stop=True),
                    waits=[rstd_rdy, (s_row, s_row.n)], inc=s_pe)
                S.op("vector", lambda e: e.tensor_copy(t_rbc[:], p_rbc[:]),
                     waits=[(s_pe, s_pe.n)], inc=s_row)
                rbc_rdy = (s_row, s_row.n)
                S.op("vector", lambda e, g=g: e.tensor_copy(
                    t_rown[:, 128 * g:128 * g + 128], R_RSTD[:, 0:128]),
                    waits=[rstd_rdy], inc=s_row)
                S.op("vector", lambda e, g=g: e.tensor_copy(
                    t_nmown[:, 128 * g:128 * g + 128], t_nmrow[:, 0:128]),
                    waits=[nm_rdy], inc=s_row)

                for c in range(NCH):
                    S.op("vector", lambda e, c=c, g=g, xts=xts: e.tensor_copy(
                        t_xtown[:, 2048 * c + 128 * g:2048 * c + 128 * g + 128],
                        xts[:, 512 * c:512 * c + 128]),
                        waits=[(sx, sx.n)], inc=s_ev)

                for mc in range(4):
                    pp = p_proj[mc]
                    S.op("tensor", lambda e, mc=mc, pp=pp: e.matmul(
                        pp[:], t_wsumkv[:, 128 * mc:128 * mc + 128], t_nmrow[:],
                        start=False, stop=True),
                        waits=[nm_rdy], inc=s_pe)
                    mm_done = (s_pe, s_pe.n)
                    S.op("vector", lambda e, mc=mc, pp=pp, kvs=kvs: e.tensor_mul(
                        kvs[:, 512 * mc:512 * mc + 512], pp[:], t_rbc[:]),
                        waits=[mm_done, rbc_rdy], inc=s_ev)
                kv_evac = (s_ev, s_ev.n)

                for mc in range(2):
                    S.op("tensor", lambda e, mc=mc, kvs=kvs: e.matmul(
                        p_sw[:], t_p2[:], kvs[:, 512 * mc:512 * mc + 512],
                        start=True, stop=True),
                        waits=[kv_evac, (s_ev, s_ev.n)], inc=s_pe)
                    sw_done = (s_pe, s_pe.n)
                    S.op("vector", lambda e, mc=mc, kvs=kvs: e.tensor_mul(
                        t_tmp1[:], kvs[:, 512 * mc:512 * mc + 512], t_csk[:, 0:512]),
                        waits=[kv_evac, csk_rdy])
                    S.op("vector", lambda e: e.scalar_tensor_tensor(
                        t_tmp2[:], p_sw[:], 1.0, t_csk[:, 512:1024],
                        ALU.mult, ALU.mult),
                        waits=[sw_done])
                    S.op("vector", lambda e, mc=mc, kvs=kvs: e.tensor_add(
                        kvs[:, 512 * mc:512 * mc + 512], t_tmp1[:], t_tmp2[:]),
                        inc=s_ev)
                rope_done = (s_ev, s_ev.n)
                csk_used = s_ev.n

                if g < 14:
                    for mc in range(2):
                        S.op("vector", lambda e, mc=mc, g=g, kvs=kvs: e.tensor_reduce(
                            t_k2f[mc][:, 8 * g:8 * g + 8],
                            kvs[:, 512 * mc:512 * mc + 512].rearrange(
                                "p (a b) -> p a b", b=64), AX, ALU.add),
                            waits=[rope_done], inc=s_ev)
                    for mc in range(2, 4):
                        S.op("vector", lambda e, mc=mc, g=g, kvs=kvs: e.tensor_reduce(
                            t_v2f[mc - 2][:, 8 * g:8 * g + 8],
                            kvs[:, 512 * mc:512 * mc + 512].rearrange(
                                "p (a b) -> p a b", b=64), AX, ALU.add),
                            waits=[kv_evac], inc=s_ev)
                else:
                    gw = g - 14
                    for mc in range(2):
                        S.op("vector", lambda e, mc=mc, gw=gw, kvs=kvs: e.tensor_copy(
                            t_k2[mc][:, KEYPAD + 512 * gw:KEYPAD + 512 * gw + 512],
                            kvs[:, 512 * mc:512 * mc + 512]),
                            waits=[rope_done], inc=s_ev)
                    for mc in range(2, 4):
                        for i in range(4):
                            S.op("tensor", lambda e, mc=mc, i=i, kvs=kvs: e.matmul(
                                p_sw[:, 0:128],
                                kvs[:, 512 * mc + 128 * i:512 * mc + 128 * i + 128],
                                t_ident[:], start=True, stop=True),
                                waits=[kv_evac, (s_ev, s_ev.n), (s_evS, s_evS.n)],
                                inc=s_pe)
                            S.op("scalar", lambda e, mc=mc, i=i, gw=gw: e.copy(
                                t_v2km[1 + 4 * gw + i][:, 128 * (mc - 2):128 * (mc - 2) + 128],
                                p_sw[:, 0:128]),
                                waits=[(s_pe, s_pe.n)], inc=s_evS)

                hist.append((s_pe.n, s_ev.n, s_sq.n, s_sqm.n))

            pool_done = (s_ev, s_ev.n)
            for mc in range(2):
                S.op("vector", lambda e, mc=mc: e.tensor_scalar(
                    t_k2[mc][:, 0:NPOOL], t_k2f[mc][:], 1.0 / 64.0, None, ALU.mult),
                    waits=[pool_done], inc=s_ev)
                S.op("vector", lambda e, mc=mc: e.tensor_scalar(
                    t_v2ps[mc][:], t_v2f[mc][:], 1.0 / 64.0, None, ALU.mult),
                    waits=[pool_done], inc=s_ev)
            v2ps_rdy = (s_ev, s_ev.n)
            for mc in range(2):
                S.op("tensor", lambda e, mc=mc: e.matmul(
                    p_sw[0:NPOOL, 0:128], t_v2ps[mc][:, 0:NPOOL], t_ident[:],
                    start=True, stop=True),
                    waits=[v2ps_rdy, (s_ev, s_ev.n), (s_evS, s_evS.n)], inc=s_pe)
                S.op("scalar", lambda e, mc=mc: e.copy(
                    t_v2km[0][0:NPOOL, 128 * mc:128 * mc + 128],
                    p_sw[0:NPOOL, 0:128]),
                    waits=[(s_pe, s_pe.n)], inc=s_evS)

            S.barrier([s_pe, s_ev, s_evS, s_sq, s_sqm, s_row, s_rowS,
                       s_c0, s_wkv, s_xtsS[0], s_xtsS[1], s_cskS, s_dbgst])
        es_kv.close()

        t_qstk = [es_qs.enter_context(
            nc.sbuf_tensor(f"sb_qstk_{i}", [128, 2048], BF16, side="right"))
            for i in range(8)]
        with contextlib.ExitStack() as phq:
            sbq = lambda name, shape, dt=F32: phq.enter_context(
                nc.sbuf_tensor("sb_" + name, shape, dt))
            pbq = lambda name, shape: phq.enter_context(
                nc.psum_tensor(name, shape, F32))
            t_wqs = sbq("wqs", [128, NCH * 1024], BF16)
            t_rbcq = sbq("rbcq", [128, 2048])
            t_wsq = sbq("wsq", [1, 128], BF16)
            t_csq = sbq("csq", [128, 1024])
            t_tq1 = sbq("tq1", [128, 512])
            t_tq2 = sbq("tq2", [128, 512])
            p_q = [pbq(f"p_q{i}", [128, 512]) for i in range(4)]
            p_wsq = pbq("p_wsq", [1, 128])
            p_qsw = pbq("p_qsw", [128, 512])

            for c in range(NCH):
                dma("gpsimd", t_wqs[:, 1024 * c:1024 * c + 1024],
                    d_wq[128 * c:128 * c + 128, :], s_wqL)
            wq_rdy = (s_wqL, s_wqL.n)

            for ns in range(4):
                S.op("tensor", lambda e, ns=ns: e.matmul(
                    p_q[ns][:], t_onesr[:], t_rown[:, 512 * ns:512 * ns + 512],
                    start=True, stop=True), inc=s_pe)
            rq_mm = (s_pe, s_pe.n)
            for ns in range(4):
                S.op("vector", lambda e, ns=ns: e.tensor_copy(
                    t_rbcq[:, 512 * ns:512 * ns + 512], p_q[ns][:]),
                    waits=[rq_mm], inc=s_ev)
            rbcq_rdy = (s_ev, s_ev.n)

            csq_used = 0
            for mc in range(8):
                for c in range(NCH):
                    S.op("tensor", lambda e, mc=mc, c=c: e.matmul(
                        p_wsq[:], t_onesc[:],
                        t_wqs[:, 1024 * c + 128 * mc:1024 * c + 128 * mc + 128],
                        start=(c == 0), stop=(c == NCH - 1)),
                        waits=([wq_rdy, (s_ev, s_ev.n)] if c == 0 else []),
                        inc=(s_pe if c == NCH - 1 else None))
                wsq_mm = (s_pe, s_pe.n)
                S.op("vector", lambda e: e.tensor_copy(t_wsq[:], p_wsq[:]),
                     waits=[wsq_mm], inc=s_ev)
                wsq_rdy = (s_ev, s_ev.n)
                ev_before = s_ev.n
                for ns in range(4):
                    for c in range(NCH):
                        S.op("tensor", lambda e, mc=mc, ns=ns, c=c: e.matmul(
                            p_q[ns][:],
                            t_wqs[:, 1024 * c + 128 * mc:1024 * c + 128 * mc + 128],
                            t_xtown[:, 2048 * c + 512 * ns:2048 * c + 512 * ns + 512],
                            start=(c == 0), stop=False),
                            waits=([rbcq_rdy, wsq_rdy, (s_ev, ev_before)]
                                   if (ns == 0 and c == 0) else []))
                    S.op("tensor", lambda e, ns=ns: e.matmul(
                        p_q[ns][:], t_wsq[:], t_nmown[:, 512 * ns:512 * ns + 512],
                        start=False, stop=True), inc=s_pe)
                qmm_done = (s_pe, s_pe.n)
                for ns in range(4):
                    S.op("vector", lambda e, mc=mc, ns=ns: e.tensor_mul(
                        t_qstk[mc][:, 512 * ns:512 * ns + 512], p_q[ns][:],
                        t_rbcq[:, 512 * ns:512 * ns + 512]),
                        waits=[qmm_done], inc=s_ev)
                qev = (s_ev, s_ev.n)
                for ns in range(4):
                    dma("sync", t_csq[:, 0:512],
                        d_cosq[:, 512 * ns:512 * ns + 512], s_csqS,
                        waits=[(s_ev, csq_used)])
                    dma("sync", t_csq[:, 512:1024],
                        d_sinq[:, 512 * ns:512 * ns + 512], s_csqS)
                    csq_rdy = (s_csqS, s_csqS.n)
                    S.op("tensor", lambda e, mc=mc, ns=ns: e.matmul(
                        p_qsw[:], t_p2[:], t_qstk[mc][:, 512 * ns:512 * ns + 512],
                        start=True, stop=True),
                        waits=[qev, (s_ev, s_ev.n)], inc=s_pe)
                    qsw = (s_pe, s_pe.n)
                    S.op("vector", lambda e, mc=mc, ns=ns: e.tensor_mul(
                        t_tq1[:], t_qstk[mc][:, 512 * ns:512 * ns + 512],
                        t_csq[:, 0:512]),
                        waits=[qev, csq_rdy])
                    S.op("vector", lambda e: e.scalar_tensor_tensor(
                        t_tq2[:], p_qsw[:], 1.0, t_csq[:, 512:1024],
                        ALU.mult, ALU.mult),
                        waits=[qsw])
                    S.op("vector", lambda e, mc=mc, ns=ns: e.tensor_add(
                        t_qstk[mc][:, 512 * ns:512 * ns + 512], t_tq1[:], t_tq2[:]),
                        inc=s_ev)
                    csq_used = s_ev.n

            S.barrier([s_pe, s_ev, s_evS, s_wqL, s_csqS])
        es_xt.close()

        with contextlib.ExitStack() as pha:
            sba = lambda name, shape, dt=F32: pha.enter_context(
                nc.sbuf_tensor("sb_" + name, shape, dt))
            pba = lambda name, shape: pha.enter_context(
                nc.psum_tensor(name, shape, F32))
            t_wo = sba("wo_sb", [128, 8 * 2048], BF16)
            t_mask = [sba(f"mask{i}", [128, 1536]) for i in range(2)]
            t_xot = [sba(f"xot{i}", [128, 2048]) for i in range(2)]
            t_ssb = [sba(f"ssb{i}", [128, 1536]) for i in range(2)]
            t_pb = [sba(f"pb{i}", [128, 1536], BF16) for i in range(2)]
            t_pn = [sba(f"pn{i}", [128, 1536], BF16) for i in range(2)]
            t_pt = [sba(f"pt{i}", [128, 128], BF16) for i in range(2)]
            t_att = [sba(f"att{i}", [128, 128], BF16) for i in range(8)]
            t_nm = sba("nm", [128, 4])
            t_sumc = sba("sumc", [128, 4])
            t_rsc = sba("rsc", [128, 4])
            t_yt = [sba(f"yt{i}", [128, 2048]) for i in range(2)]
            t_l2 = sba("l2", [128, 12])
            t_qxf = sba("qxf", [128, 2048])
            t_qxb = sba("qxb", [128, 2048], BF16)
            t_qxt = sba("qxtb", [128, 2048], BF16)
            t_w1n = sba("w1n", [128, 2048])
            t_w1am = sba("w1am", [128, NW1])
            t_w2n = sba("w2n", [128, I_FF])
            t_w2am = sba("w2am", [128, 16])
            t_wrow = sba("wrow", [1, 128])
            t_dbgsb = sba("dbgsb", [128, 16 * 13]) if stage == "debug" else None
            p_sc = pba("p_sc", [128, 1536])
            p_pt = [pba(f"p_pt{i}", [128, 128]) for i in range(2)]
            p_att = pba("p_att", [128, 128])
            p_wo = [pba(f"p_wo{i}", [128, 512]) for i in range(2)]

            for c in range(8):
                dma("gpsimd", t_wo[:, 2048 * c:2048 * c + 2048],
                    d_wo[128 * c:128 * c + 128, :], s_woL)
            wo_rdy = (s_woL, s_woL.n)
            if d_dbg2 is not None:
                off = 0
                for tt_ in t_k2 + t_v2km + t_qstk:
                    w_ = tt_.shape[1]
                    S.op("sync", lambda e, tt_=tt_, off=off, w_=w_: e.dma_start(
                        out=d_dbg2[:, off:off + w_], in_=tt_[:]),
                        inc=s_dbgst, amt=16)
                    off += w_

            (C_SUM, C_SSQ, C_MEAN, C_VEPS, C_RCP, C_RSTD, C_YMX, C_YMN,
             C_AMX, C_XSC, C_IXS, C_A) = range(12)

            mask_used = [0, 0]
            xot_used = [0, 0]
            ssb_used = [0, 0]
            pn_pe_used = [0, 0]
            nm_used = [0, 0, 0, 0]
            ptb_used = [0, 0]
            att_ev = 0
            wo_used = [0, 0]
            yt_used = [(0, 0, 0), (0, 0, 0)]
            qxt_st = 0
            qxf_ev = 0

            wtiles = [("w1", it) for it in range(NW1)] + \
                     [("w2", it) for it in range(16)]
            wpos = 0
            wbuf_used = 0

            def sched_wsc(n):
                nonlocal wpos
                for _ in range(n):
                    if wpos >= len(wtiles):
                        return
                    kind, it = wtiles[wpos]
                    wpos += 1
                    if kind == "w1":
                        buf, am, src = t_w1n, t_w1am, d_w1nat[128 * it:128 * it + 128, :]
                        dst = am[:, it:it + 1]
                    else:
                        buf, am, src = t_w2n, t_w2am, d_w2nat[128 * it:128 * it + 128, :]
                        dst = am[:, it:it + 1]
                    S.op("sync", lambda e, buf=buf, src=src: e.dma_start(
                        out=buf[:], in_=src),
                        waits=[(s_wsc, s_wsc.n)], inc=s_wscL, amt=16)
                    S.op("vector", lambda e, buf=buf, dst=dst: e.tensor_reduce(
                        dst, buf[:], AX, ALU.max, apply_absolute_value=True),
                        waits=[(s_wscL, s_wscL.n)], inc=s_wsc)

            for m in range(NSLOT):
                par = m % 2
                P, W = _pm(m), _wm(m)
                KS = KSTORE[m]
                nwin = W // 512
                chunks = [(0, 0, P)]
                for wci in range(W // 128):
                    chunks.append((512 + 128 * wci, 1 + wci, 128))
                mask, xot = t_mask[par], t_xot[par]
                dma("sync", mask[:, 0:KS],
                    d_masks[:, int(MOFF[m]):int(MOFF[m]) + KS], s_maskS[par],
                    waits=([(s_ev, mask_used[par])] if m >= 2 else []))
                mask_rdy = (s_maskS[par], s_maskS[par].n)
                dma("sync", xot[:], d_xown[128 * m:128 * m + 128, :], s_xotS[par],
                    waits=([(s_ev, xot_used[par])] if m >= 2 else []))
                xot_rdy = (s_xotS[par], s_xotS[par].n)

                gap_rdy = []
                if W > 0:
                    S.op("vector", lambda e, P=P: e.memset(p_sc[:, P:512], 0.0),
                         waits=[(s_pe, s_pe.n)], inc=s_ev)
                    gap_rdy = [(s_ev, s_ev.n)]
                small = (W == 0)
                scb_used = [s_ev.n, s_ev.n, s_ev.n]
                def head_stage1(hh):
                    h = HEADS[hh]
                    hp = hh % 2
                    kv = h // 4
                    kc, kp = kv // 2, 64 * (kv % 2)
                    qc, qp = hh // 2, 64 * (hh % 2)
                    assert kp == qp
                    ko = 512 * (hh % 3) if small else 0
                    scw = ([(s_ev, scb_used[hh % 3])] if small
                           else [(s_ev, max(scb_used))] + gap_rdy)
                    for wi in range(-1, nwin):
                        if wi < 0:
                            bld = (lambda e, m=m, P=P, qc=qc, qp=qp, kc=kc, kp=kp, ko=ko:
                                   e.matmul(p_sc[:, ko:ko + P],
                                            t_qstk[qc][qp:qp + 64, 128 * m:128 * m + 128],
                                            t_k2[kc][kp:kp + 64, 0:P],
                                            start=True, stop=True))
                        else:
                            bld = (lambda e, m=m, wi=wi, qc=qc, qp=qp, kc=kc, kp=kp:
                                   e.matmul(p_sc[:, 512 + 512 * wi:1024 + 512 * wi],
                                            t_qstk[qc][qp:qp + 64, 128 * m:128 * m + 128],
                                            t_k2[kc][kp:kp + 64,
                                                     KEYPAD + 512 * wi:KEYPAD + 512 * wi + 512],
                                            start=True, stop=True))
                        S.op("tensor", bld,
                             waits=(scw if wi < 0 else []),
                             inc=(s_pe if wi == nwin - 1 else None))
                    sc_done = (s_pe, s_pe.n)

                    ssb, pb_, pn = t_ssb[hp], t_pb[hp], t_pn[hp]
                    S.op("vector", lambda e, h=h, KS=KS, ssb=ssb, mask=mask, ko=ko:
                         e.scalar_tensor_tensor(
                             ssb[:, 0:KS], p_sc[:, ko:ko + KS], t_cbc[:, h:h + 1],
                             mask[:, 0:KS], ALU.mult, ALU.min),
                         waits=[sc_done, mask_rdy, (s_evS, ssb_used[hp])],
                         inc=s_ev)
                    scb_used[hh % 3] = s_ev.n
                    ii = hh % 4
                    S.op("vector", lambda e, KS=KS, ii=ii, ssb=ssb: e.tensor_reduce(
                        t_nm[:, ii:ii + 1], ssb[:, 0:KS], AX, ALU.max, negate=True),
                        waits=[(s_evS, nm_used[ii])], inc=s_ev)
                    nm_rdy2 = (s_ev, s_ev.n)
                    S.op("scalar", lambda e, KS=KS, ii=ii, ssb=ssb, pb_=pb_:
                         e.activation(pb_[:, 0:KS], ssb[:, 0:KS], ACT.Exp,
                                      bias=t_nm[:, ii:ii + 1], scale=1.0,
                                      accum_out=t_sumc[:, ii:ii + 1]),
                         waits=[nm_rdy2], inc=s_evS)
                    nm_used[ii] = s_evS.n
                    return dict(hh=hh, h=h, hp=hp, kv=kv, qc=qc, ii=ii,
                                exp_done=(s_evS, s_evS.n), pn=pn, pb_=pb_)

                def head_stage2a(st_):
                    hp = st_["hp"]
                    pn, pb_ = st_["pn"], st_["pb_"]
                    ii = st_["ii"]
                    S.op("vector", lambda e, ii=ii: e.reciprocal(
                        t_rsc[:, ii:ii + 1], t_sumc[:, ii:ii + 1]),
                        waits=[st_["exp_done"]], inc=s_ev)
                    rs_rdy = (s_ev, s_ev.n)
                    S.op("scalar", lambda e, KS=KS, ii=ii, pb_=pb_, pn=pn:
                         e.activation(pn[:, 0:KS], pb_[:, 0:KS], ACT.Copy,
                                      scale=t_rsc[:, ii:ii + 1]),
                         waits=[rs_rdy, (s_pe, pn_pe_used[hp])], inc=s_evS)
                    ssb_used[hp] = s_evS.n
                    st_["pn_rdy"] = (s_evS, s_evS.n)

                def head_stage2b(st_):
                    nonlocal att_ev
                    hp, kv, qc = st_["hp"], st_["kv"], st_["qc"]
                    pn = st_["pn"]
                    pn_rdy = st_["pn_rdy"]
                    npv = len(chunks)

                    def emit_pt(ci):
                        scol, vt, klen = chunks[ci]
                        cp = ci % 2
                        S.op("tensor", lambda e, scol=scol, klen=klen, cp=cp, pn=pn:
                             e.matmul(p_pt[cp][0:klen, :], pn[:, scol:scol + klen],
                                      t_ident[:], start=True, stop=True),
                             waits=[pn_rdy, (s_ev, ptb_used[cp])], inc=s_pe)
                        ptmm = (s_pe, s_pe.n)
                        S.op("vector", lambda e, klen=klen, cp=cp: e.tensor_copy(
                            t_pt[cp][0:klen, :], p_pt[cp][0:klen, :]),
                            waits=[ptmm], inc=s_ev)
                        ptb_used[cp] = s_ev.n
                        return (s_ev, s_ev.n)

                    pts = [emit_pt(ci) for ci in range(min(2, npv))]
                    for ci, (scol, vt, klen) in enumerate(chunks):
                        cp = ci % 2
                        pvw = [pts[ci]]
                        if ci == 0 and hp == 0:
                            pvw.append((s_evS, att_ev))
                        S.op("tensor", lambda e, vt=vt, klen=klen, kv=kv, hp=hp, cp=cp, ci=ci, npv=npv:
                             e.matmul(p_att[64 * hp:64 * hp + 64, :],
                                      t_v2km[vt][0:klen, 64 * kv:64 * kv + 64],
                                      t_pt[cp][0:klen, :],
                                      start=(ci == 0), stop=(ci == npv - 1)),
                             waits=pvw, inc=(s_pe if ci == npv - 1 else None))
                        if ci + 2 < npv:
                            pts.append(emit_pt(ci + 2))
                    pn_pe_used[hp] = s_pe.n
                    if hp == 1:
                        pv_done = (s_pe, s_pe.n)
                        S.op("scalar", lambda e, qc=qc: e.copy(
                            t_att[qc][:], p_att[:]),
                            waits=[pv_done], inc=s_evS)
                        att_ev = s_evS.n

                pipe = []
                for hh in range(NSLOT):
                    pipe.append(head_stage1(hh))
                    if len(pipe) >= 2:
                        head_stage2a(pipe[-2])
                    if len(pipe) >= 3:
                        head_stage2b(pipe[-3])
                head_stage2a(pipe[-1])
                head_stage2b(pipe[-2])
                head_stage2b(pipe[-1])

                sched_wsc(6)

                att_all = (s_evS, s_evS.n)
                yt = t_yt[par]
                ytu = yt_used[par]
                for ds in range(4):
                    dp = ds % 2
                    for c in range(8):
                        w0 = []
                        if c == 0:
                            w0 = [att_all, wo_rdy, (s_ev, wo_used[dp])]
                        S.op("tensor", lambda e, ds=ds, c=c, dp=dp: e.matmul(
                            p_wo[dp][:], t_att[c][:],
                            t_wo[:, 2048 * c + 512 * ds:2048 * c + 512 * ds + 512],
                            start=(c == 0), stop=(c == 7)),
                            waits=w0, inc=(s_pe if c == 7 else None))
                    womm = (s_pe, s_pe.n)
                    evw = [womm, xot_rdy]
                    if ds == 0:
                        evw += [(s_ystS[par], s_ystS[par].n), (s_qxst, s_qxst.n),
                                (s_evS, ytu[1]), (s_ev, ytu[2])]
                    S.op("vector", lambda e, ds=ds, dp=dp, yt=yt, xot=xot: e.tensor_add(
                        yt[:, 512 * ds:512 * ds + 512], p_wo[dp][:],
                        xot[:, 512 * ds:512 * ds + 512]),
                        waits=evw, inc=s_ev)
                    wo_used[dp] = s_ev.n
                y_done = (s_ev, s_ev.n)
                xot_used[par] = s_ev.n
                S.op("sync", lambda e, m=m, yt=yt: e.dma_start(
                    out=d_y[128 * m:128 * m + 128, :], in_=yt[:]),
                    waits=[y_done], inc=s_ystS[par], amt=16)
                if stage == "y":
                    S.op("sync", lambda e, m=m, yt=yt: e.dma_start(
                        out=d_out[128 * m:128 * m + 128, :], in_=yt[:]),
                        waits=[y_done], inc=s_ystS[par], amt=16)
                    mask_used[par] = s_ev.n
                    yt_used[par] = (0, s_evS.n, s_ev.n)
                    continue

                S.op("scalar", lambda e, yt=yt: e.activation(
                    t_qxf[:], yt[:], ACT.Square,
                    accum_out=t_l2[:, C_SSQ:C_SSQ + 1]),
                    waits=[y_done, (s_ev, qxf_ev)], inc=s_evS)
                S.op("scalar", lambda e, yt=yt: e.activation(
                    t_qxf[:], yt[:], ACT.Identity, scale=-1.0,
                    accum_out=t_l2[:, C_SUM:C_SUM + 1]),
                    inc=s_evS)
                stat_s = (s_evS, s_evS.n)
                S.op("vector", lambda e, yt=yt: e.tensor_reduce(
                    t_l2[:, C_YMX:C_YMX + 1], yt[:], AX, ALU.max),
                    waits=[y_done], inc=s_ev)
                S.op("vector", lambda e: e.tensor_reduce(
                    t_l2[:, C_YMN:C_YMN + 1], t_qxf[:], AX, ALU.max),
                    waits=[stat_s], inc=s_ev)
                red_done = (s_ev, s_ev.n)

                def vt(build, extra=()):
                    S.op("vector", build,
                         waits=[(s_ev, s_ev.n)] + list(extra), inc=s_ev)

                vt(lambda e: e.tensor_scalar(
                    t_l2[:, C_MEAN:C_MEAN + 1], t_l2[:, C_SUM:C_SUM + 1],
                    -1.0 / D, None, ALU.mult), extra=[stat_s, red_done])
                vt(lambda e: e.tensor_tensor(
                    t_l2[:, C_VEPS:C_VEPS + 1], t_l2[:, C_MEAN:C_MEAN + 1],
                    t_l2[:, C_MEAN:C_MEAN + 1], ALU.mult))
                vt(lambda e: e.scalar_tensor_tensor(
                    t_l2[:, C_VEPS:C_VEPS + 1], t_l2[:, C_SSQ:C_SSQ + 1],
                    1.0 / D, t_l2[:, C_VEPS:C_VEPS + 1], ALU.mult, ALU.subtract))
                vt(lambda e: e.tensor_scalar(
                    t_l2[:, C_VEPS:C_VEPS + 1], t_l2[:, C_VEPS:C_VEPS + 1],
                    LN_EPS, None, ALU.add))
                vt(lambda e: e.reciprocal(
                    t_l2[:, C_RCP:C_RCP + 1], t_l2[:, C_VEPS:C_VEPS + 1]))
                S.op("scalar", lambda e: e.activation(
                    t_l2[:, C_RSTD:C_RSTD + 1], t_l2[:, C_RCP:C_RCP + 1],
                    ACT.Sqrt),
                    waits=[(s_ev, s_ev.n)], inc=s_evS)
                rstd2 = (s_evS, s_evS.n)
                vt(lambda e: e.tensor_sub(
                    t_l2[:, C_AMX:C_AMX + 1], t_l2[:, C_YMX:C_YMX + 1],
                    t_l2[:, C_MEAN:C_MEAN + 1]), extra=[rstd2])
                vt(lambda e: e.tensor_add(
                    t_l2[:, C_YMN:C_YMN + 1], t_l2[:, C_MEAN:C_MEAN + 1],
                    t_l2[:, C_YMN:C_YMN + 1]))
                vt(lambda e: e.tensor_max(
                    t_l2[:, C_AMX:C_AMX + 1], t_l2[:, C_AMX:C_AMX + 1],
                    t_l2[:, C_YMN:C_YMN + 1]))
                vt(lambda e: e.tensor_mul(
                    t_l2[:, C_AMX:C_AMX + 1], t_l2[:, C_AMX:C_AMX + 1],
                    t_l2[:, C_RSTD:C_RSTD + 1]))
                vt(lambda e, m=m: e.tensor_scalar(
                    t_xs[:, m:m + 1], t_l2[:, C_AMX:C_AMX + 1],
                    1.0 / QMAX, Q_EPS, ALU.mult, ALU.max))
                vt(lambda e, m=m: e.reciprocal(
                    t_l2[:, C_IXS:C_IXS + 1], t_xs[:, m:m + 1]))
                vt(lambda e: e.tensor_mul(
                    t_l2[:, C_A:C_A + 1], t_l2[:, C_RSTD:C_RSTD + 1],
                    t_l2[:, C_IXS:C_IXS + 1]))
                vt(lambda e: e.tensor_mul(
                    t_l2[:, C_RCP:C_RCP + 1], t_l2[:, C_MEAN:C_MEAN + 1],
                    t_l2[:, C_A:C_A + 1]))
                vt(lambda e: e.tensor_scalar(
                    t_l2[:, C_RCP:C_RCP + 1], t_l2[:, C_RCP:C_RCP + 1],
                    -1.0, None, ALU.mult))
                ab_rdy = (s_ev, s_ev.n)
                if stage == "debug":
                    S.op("vector", lambda e, m=m: e.tensor_copy(
                        t_dbgsb[:, 13 * m:13 * m + 12], t_l2[:]),
                        waits=[ab_rdy], inc=s_ev)
                    S.op("vector", lambda e, m=m: e.tensor_copy(
                        t_dbgsb[:, 13 * m + 12:13 * m + 13], t_xs[:, m:m + 1]),
                        inc=s_ev)
                S.op("scalar", lambda e, yt=yt: e.activation(
                    t_qxf[:], yt[:], ACT.Identity,
                    bias=t_l2[:, C_RCP:C_RCP + 1],
                    scale=t_l2[:, C_A:C_A + 1]),
                    waits=[ab_rdy], inc=s_evS)
                qxp = (s_evS, s_evS.n)
                yt_used[par] = (0, s_evS.n, s_ev.n)
                mask_used[par] = s_ev.n
                S.op("vector", lambda e: e.tensor_scalar(
                    t_qxb[:], t_qxf[:], MAGIC, -MAGIC, ALU.add, ALU.add),
                    waits=[qxp, (s_pe, pn_pe_used[0]), (s_pe, pn_pe_used[1])],
                    inc=s_ev)
                qx_rdy = (s_ev, s_ev.n)
                qxf_ev = s_ev.n
                for c in range(NCH):
                    cp = c % 2
                    S.op("tensor", lambda e, c=c, cp=cp: e.matmul(
                        p_pt[cp][:], t_qxb[:, 128 * c:128 * c + 128],
                        t_ident[:], start=True, stop=True),
                        waits=[qx_rdy, (s_ev, ptb_used[cp])] +
                              ([(s_qxst, s_qxst.n)] if c == 0 else []),
                        inc=s_pe)
                    qtmm = (s_pe, s_pe.n)
                    S.op("vector", lambda e, c=c, cp=cp: e.tensor_copy(
                        t_qxt[:, 128 * c:128 * c + 128], p_pt[cp][:]),
                        waits=[qtmm], inc=s_ev)
                    ptb_used[cp] = s_ev.n
                S.op("sync", lambda e, m=m: e.dma_start(
                    out=d_qxt[128 * m:128 * m + 128, :], in_=t_qxt[:]),
                    waits=[(s_ev, s_ev.n)], inc=s_qxst, amt=16)

            if stage == "y":
                S.raw_wait("sync", s_ystS[0])
                S.raw_wait("sync", s_ystS[1])
                S.finalize(stack.enter_context(nc.Block()))
                return nc

            sched_wsc(len(wtiles))
            S.op("vector", lambda e: e.tensor_scalar(
                t_w1am[:], t_w1am[:], 1.0 / QMAX, Q_EPS, ALU.mult, ALU.max),
                waits=[(s_wsc, s_wsc.n)], inc=s_ev)
            S.op("vector", lambda e: e.tensor_scalar(
                t_w2am[:], t_w2am[:], 1.0 / QMAX, Q_EPS, ALU.mult, ALU.max),
                inc=s_ev)
            wsc_rdy = (s_ev, s_ev.n)
            wrow_st = 0
            for kind, n, am, drow in (("w1", NW1, t_w1am, d_ws1),
                                      ("w2", 16, t_w2am, d_ws2)):
                for cidx in range(n):
                    S.op("tensor", lambda e, am=am, cidx=cidx: e.matmul(
                        p_pt[0][0:1, :], am[:, cidx:cidx + 1], t_identf[:],
                        start=True, stop=True),
                        waits=[wsc_rdy, (s_ev, s_ev.n), (s_wrow, s_wrow.n)],
                        inc=s_pe)
                    S.op("vector", lambda e: e.tensor_copy(
                        t_wrow[:], p_pt[0][0:1, :]),
                        waits=[(s_pe, s_pe.n)], inc=s_ev)
                    S.op("sync", lambda e, drow=drow, cidx=cidx: e.dma_start(
                        out=drow[0:1, 128 * cidx:128 * cidx + 128], in_=t_wrow[:]),
                        waits=[(s_ev, s_ev.n)], inc=s_wrow, amt=16)

            if stage == "debug":
                S.op("sync", lambda e: e.dma_start(out=d_dbg, in_=t_dbgsb[:]),
                     waits=[(s_ev, s_ev.n)], inc=s_dbgst, amt=16)
            S.barrier([s_pe, s_ev, s_evS, s_wsc, s_woL, s_wscL,
                       s_maskS[0], s_maskS[1], s_xotS[0], s_xotS[1],
                       s_ystS[0], s_ystS[1], s_qxst, s_wrow, s_dbgst])
        es_qs.close()

        with contextlib.ExitStack() as phf:
            sbf = lambda name, shape, dt=F32: phf.enter_context(
                nc.sbuf_tensor("sb_" + name, shape, dt))
            pbf = lambda name, shape: phf.enter_context(
                nc.psum_tensor(name, shape, F32))
            t_qxTs = sbf("qxTs", [128, NSLOT * 2048], BF16)
            t_w1t = [sbf(f"w1t{i}", [128, 512]) for i in range(4)]
            t_qw1 = [sbf(f"qw1_{i}", [128, 512], BF16) for i in range(64)]
            t_bc = [sbf(f"bc{i}", [128, 512]) for i in range(8)]
            t_tf1a = sbf("tf1a", [128, 512])
            t_tf1b = sbf("tf1b", [128, 512])
            t_gate = sbf("gate", [128, 512])
            t_htb = [sbf(f"htb{i}", [128, 512]) for i in range(2)]
            t_w2t = sbf("w2t", [128, 1024])
            t_qw2s = sbf("qw2s", [128, 1024], BF16)
            t_ws2ib = sbf("ws2ib", [128, 2048])
            t_wr1 = sbf("wr1", [1, 1024])
            t_wr1i = sbf("wr1i", [1, 1024])
            t_wr2 = sbf("wr2", [1, 2048])
            t_dbgp = sbf("dbgp", [128, 512 * 11]) if d_dbg4 is not None else None
            p_g = [pbf(f"p_g{i}", [128, 512]) for i in range(2)]
            p_u = [pbf(f"p_u{i}", [128, 512]) for i in range(2)]
            p_bc = pbf("p_bc", [128, 512])

            for m in range(NSLOT):
                S.op("sync", lambda e, m=m: e.dma_start(
                    out=t_qxTs[:, 2048 * m:2048 * m + 2048],
                    in_=d_qxt[128 * m:128 * m + 128, :]), inc=s_qxl, amt=16)
            qxts_rdy = (s_qxl, s_qxl.n)

            dma("sync", t_wr2[:], d_ws2[0:1, :], s_wr2L)
            S.op("vector", lambda e: e.reciprocal(t_wr2[:], t_wr2[:]),
                 waits=[(s_wr2L, s_wr2L.n)], inc=s_ev)
            S.raw_wait("vector", s_ev)
            for ns in range(4):
                S.op("tensor", lambda e, ns=ns: e.matmul(
                    p_bc[:], t_onesr[:], t_wr2[:, 512 * ns:512 * ns + 512],
                    start=True, stop=True),
                    waits=[(s_ev, s_ev.n)], inc=s_pe)
                S.op("vector", lambda e, ns=ns: e.tensor_copy(
                    t_ws2ib[:, 512 * ns:512 * ns + 512], p_bc[:]),
                    waits=[(s_pe, s_pe.n)], inc=s_ev)
            ws2ib_rdy = (s_ev, s_ev.n)

            w2q = 0
            w2q_used = 0

            def sched_w2q(n):
                nonlocal w2q, w2q_used
                for _ in range(n):
                    ih2 = w2q
                    if ih2 >= 2 * NW2:
                        return
                    w2q += 1
                    ic, hh = divmod(ih2, 2)
                    S.op("sync", lambda e, ic=ic, hh=hh: e.dma_start(
                        out=t_w2t[:],
                        in_=d_w2T[128 * ic:128 * ic + 128,
                                  1024 * hh:1024 * hh + 1024]),
                        waits=([(s_ev, w2q_used)] if ih2 >= 1 else []),
                        inc=s_w2tL, amt=16)
                    ld_rdy = (s_w2tL, s_w2tL.n)
                    S.op("vector", lambda e, hh=hh: e.tensor_mul(
                        t_w2t[:], t_w2t[:], t_ws2ib[:, 1024 * hh:1024 * hh + 1024]),
                        waits=[ld_rdy, ws2ib_rdy], inc=s_ev)
                    S.op("vector", lambda e: e.tensor_scalar(
                        t_qw2s[:], t_w2t[:], MAGIC, -MAGIC, ALU.add, ALU.add),
                        waits=[(s_qw2w, s_qw2w.n)], inc=s_ev)
                    w2q_used = s_ev.n
                    S.op("gpsimd", lambda e, ic=ic, hh=hh: e.dma_start(
                        out=d_qw2[128 * ic:128 * ic + 128,
                                  1024 * hh:1024 * hh + 1024], in_=t_qw2s[:]),
                        waits=[(s_ev, s_ev.n)], inc=s_qw2w, amt=16)

            w1b_used = [0, 0, 0, 0]
            qw_used2 = [0, 0]
            gpsum_used = [0, 0]
            silu_done = 0
            prep_state = {}

            def emit_wprep_head(pair):
                pp_ = pair % 2
                bcb = 4 * pp_
                gcol = 512 * pair
                ucol = I_FF + 512 * pair
                dma("sync", t_wr1[0:1, 0:512], d_ws1[0:1, gcol:gcol + 512],
                    s_wr1L, waits=[(s_pe, prep_state.get("wr1_ev", 0))])
                dma("sync", t_wr1[0:1, 512:1024], d_ws1[0:1, ucol:ucol + 512],
                    s_wr1L)
                S.op("vector", lambda e: e.reciprocal(t_wr1i[:], t_wr1[:]),
                     waits=[(s_wr1L, s_wr1L.n),
                            (s_pe, prep_state.get("wr1_ev", 0))], inc=s_ev)
                S.raw_wait("vector", s_ev)
                for bi, (row, off) in enumerate(
                        [(t_wr1i, 0), (t_wr1i, 512), (t_wr1, 0), (t_wr1, 512)]):
                    S.op("tensor", lambda e, row=row, off=off: e.matmul(
                        p_bc[:], t_onesr[:], row[0:1, off:off + 512],
                        start=True, stop=True),
                        waits=[(s_ev, s_ev.n), (s_pe, qw_used2[pp_])], inc=s_pe)
                    S.op("vector", lambda e, bi=bi, bcb=bcb: e.tensor_copy(
                        t_bc[bcb + bi][:], p_bc[:]),
                        waits=[(s_pe, s_pe.n)], inc=s_ev)
                prep_state["wr1_ev"] = s_pe.n
                prep_state[pair] = dict(idx=0, bc_rdy=(s_ev, s_ev.n), qw_rdy=None)

            def emit_wprep_tiles(pair, n):
                st_ = prep_state[pair]
                pp_ = pair % 2
                qwb = 32 * pp_
                bcb = 4 * pp_
                gcol = 512 * pair
                ucol = I_FF + 512 * pair
                for _ in range(n):
                    t_ = st_["idx"]
                    if t_ >= 32:
                        return
                    st_["idx"] += 1
                    c, ui = divmod(t_, 2)
                    bi_ = t_ % 4
                    wb = t_w1t[bi_]
                    qw = t_qw1[qwb + 16 * ui + c]
                    colbase = gcol if ui == 0 else ucol
                    S.op("sync", lambda e, c=c, colbase=colbase, wb=wb: e.dma_start(
                        out=wb[:], in_=d_w1T[128 * c:128 * c + 128,
                                             colbase:colbase + 512]),
                        waits=([(s_ev, w1b_used[bi_])]
                               if pair * 32 + t_ >= 4 else []),
                        inc=s_w1tS[bi_], amt=16)
                    wrdy = (s_w1tS[bi_], s_w1tS[bi_].n)
                    S.op("vector", lambda e, wb=wb, ui=ui, bcb=bcb: e.tensor_mul(
                        wb[:], wb[:], t_bc[bcb + ui][:]),
                        waits=[wrdy, st_["bc_rdy"], (s_pe, qw_used2[pp_])],
                        inc=s_ev)
                    S.op("vector", lambda e, wb=wb, qw=qw: e.tensor_scalar(
                        qw[:], wb[:], MAGIC, -MAGIC, ALU.add, ALU.add),
                        inc=s_ev)
                    w1b_used[bi_] = s_ev.n
                    if st_["idx"] == 32:
                        st_["qw_rdy"] = (s_ev, s_ev.n)

            emit_wprep_head(0)
            emit_wprep_tiles(0, 32)
            ht_parity = 0
            for pair in range(NPAIR):
                pp_ = pair % 2
                qwb = 32 * pp_
                bcb = 4 * pp_
                qw_rdy = prep_state[pair]["qw_rdy"]
                sched_w2q(8)
                for sg in range(8):
                    if pair + 1 < NPAIR:
                        if sg == 0:
                            emit_wprep_head(pair + 1)
                        emit_wprep_tiles(pair + 1, 4)
                    for si in range(2):
                        s_slot = 2 * sg + si
                        for c in range(NCH):
                            w0 = []
                            if c == 0:
                                w0 = [qw_rdy, qxts_rdy, (s_ev, gpsum_used[si])]
                            S.op("tensor", lambda e, s=s_slot, c=c, si=si, qwb=qwb: e.matmul(
                                p_g[si][:],
                                t_qxTs[:, 2048 * s + 128 * c:2048 * s + 128 * c + 128],
                                t_qw1[qwb + c][:], start=(c == 0), stop=(c == NCH - 1)),
                                waits=w0)
                            S.op("tensor", lambda e, s=s_slot, c=c, si=si, qwb=qwb: e.matmul(
                                p_u[si][:],
                                t_qxTs[:, 2048 * s + 128 * c:2048 * s + 128 * c + 128],
                                t_qw1[qwb + 16 + c][:], start=(c == 0), stop=(c == NCH - 1)),
                                inc=(s_pe if c == NCH - 1 else None))
                        gu_done = (s_pe, s_pe.n)
                        S.op("vector", lambda e, si=si, bcb=bcb: e.tensor_mul(
                            t_tf1a[:], p_g[si][:], t_bc[bcb + 2][:]),
                            waits=[gu_done, (s_evS, silu_done)], inc=s_ev)
                        S.op("scalar", lambda e, s=s_slot: e.activation(
                            t_gate[:], t_tf1a[:], ACT.Silu,
                            scale=t_xs[:, s:s + 1]),
                            waits=[(s_ev, s_ev.n)],
                            inc=s_evS)
                        silu_done = s_evS.n
                        S.op("vector", lambda e, si=si, bcb=bcb: e.tensor_mul(
                            t_tf1b[:], p_u[si][:], t_bc[bcb + 3][:]),
                            inc=s_ev)
                        gpsum_used[si] = s_ev.n
                        htb = t_htb[si]
                        S.op("vector", lambda e, htb=htb: e.tensor_mul(
                            htb[:], t_gate[:], t_tf1b[:]),
                            waits=[(s_evS, silu_done),
                                   (s_htS[si], s_htS[si].n)],
                            inc=s_ev)
                        S.op("vector", lambda e, s=s_slot, pair=pair, htb=htb:
                             e.tensor_reduce(
                                 t_habs[:, NPAIR * s + pair:NPAIR * s + pair + 1],
                                 htb[:], AX, ALU.max, apply_absolute_value=True),
                             inc=s_ev)
                        S.op("sync", lambda e, s=s_slot, pair=pair, htb=htb: e.dma_start(
                            out=d_ht[128 * s:128 * s + 128,
                                     512 * pair:512 * pair + 512],
                            in_=htb[:]),
                            waits=[(s_ev, s_ev.n)], inc=s_htS[si], amt=16)
                if pair + 1 < NPAIR:
                    emit_wprep_tiles(pair + 1, 32)
                qw_used2[pp_] = s_pe.n
            sched_w2q(2 * NW2)
            if d_dbg4 is not None:
                S.op("sync", lambda e: e.dma_start(
                    out=d_dbg4[:, 0:512 * 11], in_=t_dbgp[:]),
                    waits=[(s_ev, s_ev.n)], inc=s_dbgst, amt=16)
            S.barrier([s_pe, s_ev, s_evS, s_qxl, s_wr2L, s_w2tL, s_qw2w,
                       s_wr1L, s_w1tS[0], s_w1tS[1], s_w1tS[2], s_w1tS[3],
                       s_htS[0], s_htS[1], s_dbgst])

        with contextlib.ExitStack() as ph2:
            sb2 = lambda name, shape, dt=F32: ph2.enter_context(
                nc.sbuf_tensor("sb_" + name, shape, dt))
            pb2 = lambda name, shape: ph2.enter_context(
                nc.psum_tensor(name, shape, F32))
            t_qhT = [sb2(f"qhT{i}", [128, 512], BF16) for i in range(NW2)]
            t_htr = [sb2(f"htr{i}", [128, 2816]) for i in range(2)]
            t_qhb = sb2("qhb", [128, 2816], BF16)
            t_qw2t = [sb2(f"qw2t{i}", [128, 512], BF16) for i in range(NW2)]
            t_ws2fb = sb2("ws2fb", [128, 2048])
            t_wr2f = sb2("wr2f", [1, 2048])
            t_yrd = [sb2(f"yrd{i}", [128, 512]) for i in range(4)]
            t_of = [sb2(f"of{i}", [128, 512]) for i in range(4)]
            t_t2 = [sb2(f"t2_{i}", [128, 512]) for i in range(4)]
            p_o = [pb2(f"p_o{i}", [128, 512]) for i in range(4)]
            p_t = [pb2(f"p_t{i}", [128, 128]) for i in range(2)]
            p_b2 = pb2("p_b2", [128, 512])

            for s in range(NSLOT):
                S.op("vector", lambda e, s=s: e.tensor_reduce(
                    t_hsc[:, s:s + 1], t_habs[:, NPAIR * s:NPAIR * s + NPAIR],
                    AX, ALU.max), inc=s_ev)
            S.raw_wait("vector", s_ev)
            S.op("vector", lambda e: e.tensor_mul(t_call[:], t_hsc[:], t_xs[:]),
                 inc=s_ev)
            S.op("vector", lambda e: e.tensor_scalar(
                t_hsc[:], t_call[:], 1.0 / QMAX, Q_EPS, ALU.mult, ALU.max),
                waits=[(s_ev, s_ev.n)], inc=s_ev)
            S.op("vector", lambda e: e.reciprocal(t_call[:], t_hsc[:]),
                 waits=[(s_ev, s_ev.n)], inc=s_ev)
            S.op("vector", lambda e: e.tensor_mul(t_call[:], t_call[:], t_xs[:]),
                 waits=[(s_ev, s_ev.n)], inc=s_ev)
            csc_rdy = (s_ev, s_ev.n)

            dma("sync", t_wr2f[:], d_ws2[0:1, :], s_wr2L)
            wr2_rdy = (s_wr2L, s_wr2L.n)
            for ns in range(4):
                S.op("tensor", lambda e, ns=ns: e.matmul(
                    p_b2[:], t_onesr[:], t_wr2f[0:1, 512 * ns:512 * ns + 512],
                    start=True, stop=True),
                    waits=[wr2_rdy, (s_ev, s_ev.n)], inc=s_pe)
                S.op("vector", lambda e, ns=ns: e.tensor_copy(
                    t_ws2fb[:, 512 * ns:512 * ns + 512], p_b2[:]),
                    waits=[(s_pe, s_pe.n)], inc=s_ev)
            ws2fb_rdy = (s_ev, s_ev.n)

            htr_used = [0, 0]
            qht_used = [0, 0]
            qhb_pe = 0
            qw2_used = 0
            opsum_used = [0, 0, 0, 0]
            yrd_used = [0, 0, 0, 0]
            of_used = [0, 0, 0, 0]

            for quarter in range(4):
                for si in range(4):
                    s = 4 * quarter + si
                    for ih in range(2):
                        htb = t_htr[(2 * si + ih) % 2]
                        sl_ = (2 * si + ih) % 2
                        S.op("sync", lambda e, s=s, ih=ih, htb=htb: e.dma_start(
                            out=htb[:],
                            in_=d_ht[128 * s:128 * s + 128,
                                     2816 * ih:2816 * ih + 2816]),
                            waits=([(s_ev, htr_used[sl_])]
                                   if quarter * 8 + 2 * si + ih >= 2 else []),
                            inc=s_htrS[sl_], amt=16)
                        htr_rdy = (s_htrS[sl_], s_htrS[sl_].n)
                        S.op("scalar", lambda e, s=s, htb=htb: e.activation(
                            htb[:], htb[:], ACT.Copy, scale=t_call[:, s:s + 1]),
                            waits=[htr_rdy, csc_rdy], inc=s_evS)
                        S.op("vector", lambda e, htb=htb: e.tensor_scalar(
                            t_qhb[:], htb[:], MAGIC, -MAGIC, ALU.add, ALU.add),
                            waits=[(s_evS, s_evS.n), (s_pe, qhb_pe)], inc=s_ev)
                        htr_used[(2 * si + ih) % 2] = s_ev.n
                        qh_rdy = (s_ev, s_ev.n)
                        for icl in range(22):
                            ic = 22 * ih + icl
                            cp = ic % 2
                            S.op("tensor", lambda e, icl=icl, cp=cp: e.matmul(
                                p_t[cp][:], t_qhb[:, 128 * icl:128 * icl + 128],
                                t_ident[:], start=True, stop=True),
                                waits=[qh_rdy, (s_ev, qht_used[cp])], inc=s_pe)
                            qtm = (s_pe, s_pe.n)
                            S.op("vector", lambda e, ic=ic, si=si, cp=cp: e.tensor_copy(
                                t_qhT[ic][:, 128 * si:128 * si + 128], p_t[cp][:]),
                                waits=[qtm], inc=s_ev)
                            qht_used[cp] = s_ev.n
                        qhb_pe = s_pe.n
                qhT_rdy = (s_ev, s_ev.n)

                for o2s in range(4):
                    for ic in range(NW2):
                        S.op("sync", lambda e, ic=ic, o2s=o2s: e.dma_start(
                            out=t_qw2t[ic][:],
                            in_=d_qw2[128 * ic:128 * ic + 128,
                                      512 * o2s:512 * o2s + 512]),
                            waits=([(s_pe, qw2_used)]
                                   if (quarter * 4 + o2s >= 1 and ic == 0) else []),
                            inc=s_qw2L, amt=16)
                    qw2t_rdy = (s_qw2L, s_qw2L.n)
                    for ic in range(NW2):
                        for s4 in range(4):
                            w0 = []
                            if ic == 0:
                                w0 = [qhT_rdy, qw2t_rdy, (s_ev, opsum_used[s4])]
                            S.op("tensor", lambda e, ic=ic, s4=s4: e.matmul(
                                p_o[s4][:],
                                t_qhT[ic][:, 128 * s4:128 * s4 + 128],
                                t_qw2t[ic][:],
                                start=(ic == 0), stop=(ic == NW2 - 1)),
                                waits=w0,
                                inc=(s_pe if (ic == NW2 - 1 and s4 == 3)
                                     else None))
                    o_done = (s_pe, s_pe.n)
                    qw2_used = s_pe.n
                    for s4 in range(4):
                        s = 4 * quarter + s4
                        yb = t_yrd[s4]
                        S.op("sync", lambda e, s=s, o2s=o2s, yb=yb: e.dma_start(
                            out=yb[:],
                            in_=d_y[128 * s:128 * s + 128,
                                    512 * o2s:512 * o2s + 512]),
                            waits=([(s_ev, yrd_used[s4])]
                                   if quarter * 4 + o2s >= 1 else []),
                            inc=s_yrdS[s4], amt=16)
                        y_rdy = (s_yrdS[s4], s_yrdS[s4].n)
                        tb = t_t2[s4]
                        S.op("vector", lambda e, s4=s4, o2s=o2s, tb=tb:
                             e.tensor_mul(
                                 tb[:], p_o[s4][:],
                                 t_ws2fb[:, 512 * o2s:512 * o2s + 512]),
                             waits=[o_done, ws2fb_rdy], inc=s_ev)
                        opsum_used[s4] = s_ev.n
                        ob = t_of[s4]
                        S.op("vector", lambda e, s=s, yb=yb, tb=tb, ob=ob:
                             e.scalar_tensor_tensor(
                                 ob[:], tb[:], t_hsc[:, s:s + 1], yb[:],
                                 ALU.mult, ALU.add),
                             waits=[y_rdy, (s_outS[s4], s_outS[s4].n)],
                             inc=s_ev)
                        yrd_used[s4] = s_ev.n
                        S.op("sync", lambda e, s=s, o2s=o2s, ob=ob: e.dma_start(
                            out=d_out[128 * s:128 * s + 128,
                                      512 * o2s:512 * o2s + 512],
                            in_=ob[:]),
                            waits=[(s_ev, s_ev.n)], inc=s_outS[s4], amt=16)

            for s4 in range(4):
                S.raw_wait("sync", s_outS[s4])

        S.finalize(stack.enter_context(nc.Block()))
    return nc



def kernel(**inputs):
    in_maps = _host_prepare(inputs)
    nc = build_program("full")
    res = run_bass_kernel_spmd(nc, in_maps, core_ids=list(range(8)))
    out = np.empty((B, T, D), np.float32)
    for core in range(8):
        b, j = divmod(core, 4)
        o = res.results[core]["out"]
        for m in range(NSLOT):
            t = 4 * m + j
            out[b, 128 * t:128 * t + 128, :] = o[128 * m:128 * m + 128, :]
    return out
